# version 05
# baseline (speedup 1.0000x reference)
import contextlib
import numpy as np
import ml_dtypes
import concourse.bass as bass
import concourse.mybir as mybir
import concourse.bass_isa as bass_isa
from concourse.bass_utils import run_bass_kernel_spmd
from concourse.alu_op_type import AluOpType as OP

F32 = mybir.dt.float32
BF16 = mybir.dt.bfloat16
U8 = mybir.dt.uint8
AF = mybir.ActivationFunctionType
AXX = mybir.AxisListType.X

B, T, C = 2, 16384, 2048
DBG_TILE = 1
INNER = 8192
NCORE = 8
TOK = T // 4
NT = TOK // 128
GT = 2
NG = NT // GT
SCALE = float(np.float32(1.0 / np.sqrt(128.0)))
EPS_LN = 1e-5
EPS_RMS = 1e-6
EPS_Q = 1e-5
CD1 = float(np.float32(2048.0 ** -0.5))
CD2 = float(np.float32(8192.0 ** -0.5))
NEG = -1000000000.0
_tp = 2.0 * np.pi
CW1 = float(np.float32(_tp)); CW2 = float(np.float32(_tp - CW1)); CW3 = float(np.float32(_tp - CW1 - CW2))
HP1 = float(np.float32(np.pi / 2)); HP2 = float(np.float32(np.pi / 2 - HP1))
PI32 = float(np.float32(np.pi))
INV2PI = float(np.float32(1.0 / _tp))

ENGS = ("T", "V", "A", "G", "S")

SMALL_KEYS = {"vh16", "vs16", "s1", "s2", "onec", "oner", "aw1", "aw2", "vt2",
              "ang", "angw", "angc", "t64", "sinv", "cosv", "iot", "v14", "v15",
              "rst", "r1e", "r2e", "b2e",
              "v6", "v7", "v8", "v9", "v10", "v11", "v12", "v13",
              "m0", "m1", "q0", "q1", "mr0", "mr1", "al0", "al1", "rc0", "rc1"}


class Prog:
    def __init__(self):
        self.ops = {e: [] for e in ENGS}
        self.cnt = {e: 0 for e in ENGS}
        self.bufs = {}

    def op(self, eng, fn, reads=(), writes=(), inc=None, small=False):
        small = small or any(w in SMALL_KEYS for w in writes)
        waits = []
        swaits = []
        for b in reads:
            st = self.bufs.get(b)
            if st and st[0] is not None:
                (waits if st[0][0] != eng else (swaits if st[2] or small else [])).append(st[0])
        for b in writes:
            st = self.bufs.get(b)
            if st:
                if st[0] is not None:
                    (waits if st[0][0] != eng else (swaits if st[2] or small else [])).append(st[0])
                for rv in st[1]:
                    (waits if rv[0] != eng else (swaits if small else [])).append(rv)
        amt = inc if inc is not None else 1
        wmax = {}
        for (we, wc) in waits:
            if we == eng:
                continue
            wmax[we] = max(wmax.get(we, 0), wc)
        for (we, wc) in swaits:
            wmax[we] = max(wmax.get(we, 0), wc)
        self.cnt[eng] += amt
        ev = (eng, self.cnt[eng])
        self.ops[eng].append((sorted(wmax.items()), fn, amt))
        for b in writes:
            self.bufs[b] = [ev, [], small]
        for b in reads:
            st = self.bufs.setdefault(b, [None, [], False])
            st[1].append(ev)
        return ev

    def raw_wait(self, eng, ev):
        self.ops[eng].append(([ev], None, 0))

    def replay(self, eng, name, sems):
        waited = {}
        for waits, fn, amt in self.ops[name]:
            for (we, wc) in waits:
                if waited.get(we, 0) >= wc:
                    continue
                waited[we] = wc
                eng.wait_ge(sems[we], wc)
            if fn is not None:
                ins = fn(eng)
                if amt:
                    ins.then_inc(sems[name], amt)


def build_core(dbg=False):
    nc = bass.Bass("TRN2", target_bir_lowering=False)
    dbg_outs = []

    def din(name, shape, dty=F32):
        return nc.dram_tensor(name, shape, dty, kind="ExternalInput").ap()

    xs = din("xs", [TOK, C])
    xpre = din("xpre", [128, C])
    latq = din("latq", [64, C])
    wk = din("wk", [C, C]); wv = din("wv", [C, C])
    wg = din("wg", [C, 1024]); wp = din("wp", [C, C])
    w1 = din("w1", [INNER, C]); b1 = din("b1", [1, INNER])
    w2 = din("w2", [C, INNER]); b2 = din("b2", [1, C])
    idf_i = din("idf", [128, 128])
    mask_i = din("masku", [64, 128], U8)

    y = nc.dram_tensor("y", [TOK, C], F32, kind="ExternalOutput").ap()

    scrWg = nc.dram_tensor("scrWg", [8, 128, 4096], BF16, kind="Internal").ap()
    scrWp = nc.dram_tensor("scrWp", [16, 128, 4096], BF16, kind="Internal").ap()
    scrW1 = nc.dram_tensor("scrW1", [8, 4, 128, 5120], BF16, kind="Internal").ap()
    scrW2 = nc.dram_tensor("scrW2", [32, 128, 4096], BF16, kind="Internal").ap()

    es = contextlib.ExitStack()
    with es:
        def sb(name, shape, dty=F32):
            return es.enter_context(nc.sbuf_tensor(name, shape, dty))

        h1 = sb("h1", [128, INNER])
        h2 = sb("h2", [128, INNER])
        ring = sb("ring", [128, 2, 5120], BF16)
        bt1 = sb("bt1", [128, INNER], BF16)
        bt2 = sb("bt2", [128, INNER], BF16)
        xb = [sb("xb1", [128, C]), sb("xb2", [128, C])]
        scrA = sb("scrA", [128, C])
        scrB = sb("scrB", [128, C])
        ltH = [sb("lt1h", [128, C], BF16), sb("lt2h", [128, C], BF16)]
        ltL = [sb("lt1l", [128, C], BF16), sb("lt2l", [128, C], BF16)]
        wtH = [sb("wt1h", [128, C], BF16), sb("wt2h", [128, C], BF16)]
        wtL = [sb("wt1l", [128, C], BF16), sb("wt2l", [128, C], BF16)]
        xqt = [sb("xqt1", [128, C], BF16), sb("xqt2", [128, C], BF16)]
        gth = xqt[0][0:64, :]
        gtl = xqt[1][0:64, :]
        atnh = sb("atnh", [64, C], BF16)
        atnl = sb("atnl", [64, C], BF16)
        b2ext = sb("b2ext", [4, C], BF16)
        idf = sb("idf_s", [128, 128])
        masku = sb("masku_s", [64, 128], U8)
        r1ext = [sb("r1ext0", [4, 128], BF16), sb("r1ext1", [4, 128], BF16)]
        r2ext = [sb("r2ext0", [4, 128], BF16), sb("r2ext1", [4, 128], BF16)]
        vecs = sb("vecs", [128, 24])
        ones32 = sb("ones32", [1, 128])
        vh16 = sb("vh16", [128, 16])
        vs16 = sb("vs16", [128, 16])
        pA = es.enter_context(nc.psum_tensor("pA", [128, 2048], F32))
        pB = es.enter_context(nc.psum_tensor("pB", [128, 2048], F32))
        sems = {e: es.enter_context(nc.semaphore("sem" + e)) for e in ENGS}
        block = es.enter_context(nc.Block())

        p = Prog()
        htiles = [h1, h2]
        hkeys = ["h1", "h2"]
        xkeys = ["x0", "x1"]
        btT = [bt1, bt2]
        btkeys = ["bt1", "bt2"]
        psums = [pA, pB]
        pkeys = ["pA", "pB"]

        def vc(j):
            return vecs[:, j:j + 1]

        def dma(dst, src, reads=(), writes=(), q="S"):
            return p.op(q, lambda s, d=dst, r=src: s.dma_start(out=d, in_=r),
                        reads=reads, writes=writes, inc=16)

        def group(eng, fns, reads=(), writes=()):
            def f(e):
                ins = None
                for fn in fns:
                    ins = fn(e)
                return ins
            return p.op(eng, f, reads=reads, writes=writes)

        def V(fn, reads=(), writes=(), small=False):
            return p.op("V", fn, reads=reads, writes=writes, small=small)

        def A_(fn, reads=(), writes=(), small=False):
            return p.op("A", fn, reads=reads, writes=writes, small=small)

        def G_(fn, reads=(), writes=()):
            return p.op("G", fn, reads=reads, writes=writes)

        def tap(name, ap, reads, dty=F32):
            if not dbg:
                return
            t_ = nc.dram_tensor("dbg_" + name, list(ap.shape), dty, kind="ExternalOutput").ap()
            dbg_outs.append("dbg_" + name)
            dma(t_, ap, reads=reads, q="G")

        def hilo(dsth, dstl, src, kh, kl, ksrc):
            V(lambda v: v.tensor_copy(dsth, src), reads=[ksrc], writes=[kh])
            V(lambda v: v.tensor_tensor(dstl, src, dsth, op=OP.subtract),
              reads=[ksrc, kh], writes=[kl])

        def transpose16(ps, pk, src, ksrc, n=16):
            group("T", [(lambda e, j=j: e.transpose(ps[:, j * 128:(j + 1) * 128],
                                                    src[:, j * 128:(j + 1) * 128], idf[:]))
                        for j in range(n)], reads=[ksrc, "idf"], writes=[pk])

        dma(idf[:], idf_i, writes=["idf"])
        dma(masku[:], mask_i, writes=["masku"])
        dma(scrA[0:1, :], b2, writes=["scrA"])
        V(lambda v: v.tensor_copy(bt1[0:1, 0:2048], scrA[0:1, :]), reads=["scrA"], writes=["bt1"])
        V(lambda v: v.tensor_tensor(bt2[0:1, 0:2048], scrA[0:1, :], bt1[0:1, 0:2048], op=OP.subtract),
          reads=["scrA", "bt1"], writes=["bt2"])
        dma(b2ext[0:1, :], bt1[0:1, 0:2048], reads=["bt1"], writes=["b2e"], q="G")
        dma(b2ext[1:2, :], bt1[0:1, 0:2048], reads=["bt1"], writes=["b2e"], q="G")
        dma(b2ext[2:3, :], bt2[0:1, 0:2048], reads=["bt2"], writes=["b2e"], q="G")
        dma(b2ext[3:4, :], bt2[0:1, 0:2048], reads=["bt2"], writes=["b2e"], q="G")

        for kt in range(16):
            dma(scrA[:, 0:1024], wg[kt * 128:(kt + 1) * 128, :], writes=["scrA"])
            hilo(ltH[0][:, 0:1024], ltL[0][:, 0:1024], scrA[:, 0:1024], "lt0h", "lt0l", "scrA")
            dma(scrWg[kt // 2, :, (kt % 2) * 2048:(kt % 2) * 2048 + 1024], ltH[0][:, 0:1024], reads=["lt0h"])
            dma(scrWg[kt // 2, :, (kt % 2) * 2048 + 1024:(kt % 2) * 2048 + 2048], ltL[0][:, 0:1024], reads=["lt0l"])
        for kt in range(16):
            dma(scrA[:], wp[kt * 128:(kt + 1) * 128, :], writes=["scrA"])
            hilo(ltH[0][:], ltL[0][:], scrA[:], "lt0h", "lt0l", "scrA")
            dma(scrWp[kt, :, 0:2048], ltH[0][:], reads=["lt0h"])
            dma(scrWp[kt, :, 2048:4096], ltL[0][:], reads=["lt0l"])

        V(lambda v: v.memset(vc(0), 0.0), writes=["aw1"])
        V(lambda v: v.memset(vc(1), 0.0), writes=["aw2"])
        for ot in range(64):
            dma(h1[:, 0:2048], w1[ot * 128:(ot + 1) * 128, :], writes=["h1"])
            A_(lambda a: a.activation(h1[:, 2048:4096], h1[:, 0:2048], AF.Sign),
               reads=["h1"], writes=["h1s"])
            V(lambda v: v.scalar_tensor_tensor(h1[:, 4096:6144], h1[:, 0:2048], 1.0, h1[:, 2048:4096],
                                               op0=OP.mult, op1=OP.mult, accum_out=vc(2)),
              reads=["h1", "h1s"], writes=["vt2"])
            V(lambda v: v.tensor_tensor(vc(0), vc(0), vc(2), op=OP.add),
              reads=["aw1", "vt2"], writes=["aw1"])
            transpose16(pA, "pA", h1[:, 2048:4096], "h1s")
            V(lambda v: v.tensor_copy(bt1[:, 0:2048], pA[:]), reads=["pA"], writes=["bt1"])
            oc, o8 = ot >> 3, (ot & 7) * 128
            for s4 in range(4):
                dma(scrW1[oc, s4, :, :].rearrange("p (q c) -> p q c", q=5)[:, 0:4, o8:o8 + 128],
                    bt1[:, s4 * 512:(s4 + 1) * 512].rearrange("p (q c) -> p q c", q=4), reads=["bt1"])
        for ct in range(16):
            dma(h1[:, 0:8192], w2[ct * 128:(ct + 1) * 128, :], writes=["h1"])
            A_(lambda a: a.activation(h2[:, 0:8192], h1[:, 0:8192], AF.Sign),
               reads=["h1"], writes=["h2"])
            V(lambda v: v.scalar_tensor_tensor(h1[:, 0:8192], h1[:, 0:8192], 1.0, h2[:, 0:8192],
                                               op0=OP.mult, op1=OP.mult, accum_out=vc(2)),
              reads=["h1", "h2"], writes=["h1", "vt2"])
            V(lambda v: v.tensor_tensor(vc(1), vc(1), vc(2), op=OP.add),
              reads=["aw2", "vt2"], writes=["aw2"])
            for half in range(2):
                transpose16(pA, "pA", h2[:, half * 4096:half * 4096 + 2048], "h2")
                V(lambda v: v.tensor_copy(bt1[:, 0:2048], pA[:]), reads=["pA"], writes=["bt1"])
                transpose16(pA, "pA", h2[:, half * 4096 + 2048:half * 4096 + 4096], "h2")
                V(lambda v: v.tensor_copy(bt1[:, 2048:4096], pA[:]), reads=["pA"], writes=["bt1"])
                for e1 in range(2):
                    dma(scrW2[half * 16:(half + 1) * 16, :, :].rearrange("s p (e c) -> p s e c", e=2)[:, :, e1, ct * 128:ct * 128 + 128],
                        bt1[:, 0:4096].rearrange("p (s e c) -> p s e c", s=16, e=2)[:, :, e1, :], reads=["bt1"])
        dma(h1[0:1, 0:8192], b1, writes=["h1"])
        V(lambda v: v.tensor_copy(bt1[0:1, 0:8192], h1[0:1, 0:8192]), reads=["h1"], writes=["bt1"])
        V(lambda v: v.tensor_tensor(bt2[0:1, 0:8192], h1[0:1, 0:8192], bt1[0:1, 0:8192], op=OP.subtract),
          reads=["h1", "bt1"], writes=["bt2"])
        for oc in range(8):
            dma(scrW1[oc, 3, 0:1, 4096:5120], bt1[0:1, oc * 1024:(oc + 1) * 1024], reads=["bt1"])
            dma(scrW1[oc, 3, 1:2, 4096:5120], bt1[0:1, oc * 1024:(oc + 1) * 1024], reads=["bt1"])
            dma(scrW1[oc, 3, 2:3, 4096:5120], bt2[0:1, oc * 1024:(oc + 1) * 1024], reads=["bt2"])
            dma(scrW1[oc, 3, 3:4, 4096:5120], bt2[0:1, oc * 1024:(oc + 1) * 1024], reads=["bt2"])

        V(lambda v: v.memset(vc(20), 1.0), writes=["onec"])
        V(lambda v: v.memset(ones32[0:1, :], 1.0), writes=["oner"])
        for (jsrc, jk, jdst, dk) in ((0, "aw1", 4, "s1"), (1, "aw2", 5, "s2")):
            p.op("T", lambda e, jsrc=jsrc: e.matmul(pB[0:1, 0:1], vc(jsrc), vc(20), start=True, stop=True),
                 reads=[jk, "onec", "pB"], writes=["pB"])
            V(lambda v: v.tensor_copy(scrA[0:1, 0:1], pB[0:1, 0:1]), reads=["pB"], writes=["scrA"], small=True)
            p.op("T", lambda e: e.matmul(pB[:, 0:1], ones32[0:1, :], scrA[0:1, 0:1], start=True, stop=True),
                 reads=["oner", "scrA", "pB"], writes=["pB"])
            V(lambda v, jdst=jdst: v.tensor_scalar(vc(jdst), pB[:, 0:1], 1.0 / (INNER * C), None, op0=OP.mult),
              reads=["pB"], writes=[dk])

        tap("svals", vecs[:, 0:8], ["s1", "s2"])
        kpre = h1[:, 0:2048]; vpre = h1[:, 2048:4096]; lnp = h1[:, 4096:6144]
        tabC = h2[:, 0:2048]; tabS = h2[:, 2048:4096]; ksw = h2[:, 4096:6144]
        ang = h2[:, 6144:6208]; angw = h2[:, 6208:6272]; angc = h2[:, 6272:6336]
        sinv = h2[:, 6336:6400]; cosv = h2[:, 6400:6464]; iot = h2[:, 6464:6528]
        tmp64 = h2[:, 6528:6592]
        krth = bt1[:, 0:2048]; krtl = bt1[:, 2048:4096]
        qth = bt1[:, 4096:5120]; qtl = bt1[:, 5120:6144]
        wth = bt1[:, 6144:7168]; wtl = bt1[:, 7168:8192]
        vhh = bt2[:, 0:2048]; vll = bt2[:, 2048:4096]
        wkh = bt2[:, 4096:6144]; wkl = bt2[:, 6144:8192]

        dma(xb[0][:], xpre, writes=["x0"])
        dma(scrB[0:64, 0:2048], latq, writes=["scrB"])

        def ln_stats(xap, xk, jm, jq, jr, jmr):
            V(lambda v: v.tensor_reduce(vc(jm), xap, axis=AXX, op=OP.add),
              reads=[xk], writes=[f"v{jm}"])
            A_(lambda a: a.activation(scrA[:], xap, AF.Square, accum_out=vc(jq)),
               reads=[xk], writes=["scrA", f"v{jq}"])
            V(lambda v: v.tensor_scalar(vc(jm), vc(jm), 1.0 / 2048.0, None, op0=OP.mult),
              reads=[f"v{jm}"], writes=[f"v{jm}"])
            V(lambda v: v.tensor_tensor(vc(jr), vc(jm), vc(jm), op=OP.mult),
              reads=[f"v{jm}"], writes=[f"v{jr}"])
            V(lambda v: v.tensor_scalar(vc(jq), vc(jq), 1.0 / 2048.0, None, op0=OP.mult),
              reads=[f"v{jq}"], writes=[f"v{jq}"])
            V(lambda v: v.tensor_tensor(vc(jr), vc(jq), vc(jr), op=OP.subtract),
              reads=[f"v{jq}", f"v{jr}"], writes=[f"v{jr}"])
            V(lambda v: v.tensor_scalar(vc(jr), vc(jr), 1.0, EPS_LN, op0=OP.mult, op1=OP.add),
              reads=[f"v{jr}"], writes=[f"v{jr}"])
            A_(lambda a: a.activation(vc(jr), vc(jr), AF.Ln),
               reads=[f"v{jr}"], writes=[f"v{jr}"])
            A_(lambda a: a.activation(vc(jr), vc(jr), AF.Exp, scale=-0.5),
               reads=[f"v{jr}"], writes=[f"v{jr}"])
            V(lambda v: v.tensor_tensor(vc(jmr), vc(jm), vc(jr), op=OP.mult),
              reads=[f"v{jm}", f"v{jr}"], writes=[f"v{jmr}"])
            V(lambda v: v.tensor_scalar(vc(jmr), vc(jmr), -1.0, None, op0=OP.mult),
              reads=[f"v{jmr}"], writes=[f"v{jmr}"])

        def ln_apply(dst, dk, xap, xk, jr, jmr):
            V(lambda v: v.tensor_scalar(dst, xap, vc(jr), vc(jmr), op0=OP.mult, op1=OP.add),
              reads=[xk, f"v{jr}", f"v{jmr}"], writes=[dk])

        ln_stats(xb[0][:], "x0", 6, 8, 10, 12)
        ln_apply(lnp, "h1", xb[0][:], "x0", 10, 12)
        transpose16(pA, "pA", lnp, "h1")
        hilo(ltH[0][:], ltL[0][:], pA[:], "lt0h", "lt0l", "pA")

        for which, dstf, dstk in ((0, kpre, "h1"), (1, vpre, "h1")):
            wmat = wk if which == 0 else wv
            for kt in range(16):
                dma(scrA[:], wmat[kt * 128:(kt + 1) * 128, :], writes=["scrA"])
                hilo(wkh, wkl, scrA[:], "wkh", "wkl", "scrA")
                fns = []
                for n in range(4):
                    sl = slice(n * 512, (n + 1) * 512)
                    fns.append(lambda e, kt=kt, sl=sl: e.matmul(
                        pB[:, sl], ltH[0][:, kt * 128:(kt + 1) * 128], wkh[:, sl], start=(kt == 0), stop=False))
                    fns.append(lambda e, kt=kt, sl=sl: e.matmul(
                        pB[:, sl], ltH[0][:, kt * 128:(kt + 1) * 128], wkl[:, sl], start=False, stop=False))
                    fns.append(lambda e, kt=kt, sl=sl: e.matmul(
                        pB[:, sl], ltL[0][:, kt * 128:(kt + 1) * 128], wkh[:, sl], start=False, stop=(kt == 15)))
                group("T", fns, reads=["wkh", "wkl", "lt0h", "lt0l"], writes=["pB"])
            V(lambda v, dstf=dstf: v.tensor_copy(dstf, pB[:]), reads=["pB"], writes=[dstk])

        G_(lambda g: g.iota(iot, [[1, 64]], channel_multiplier=0, allow_small_or_imprecise_dtypes=True),
           writes=["h2"])
        G_(lambda g: g.iota(vc(14), [[0, 1]], channel_multiplier=1, allow_small_or_imprecise_dtypes=True),
           writes=["v14"])
        V(lambda v: v.memset(vc(15), 10000.0), writes=["v15"])
        A_(lambda a: a.activation(vc(15), vc(15), AF.Ln), reads=["v15"], writes=["v15"])
        V(lambda v: v.tensor_scalar(ang, iot, vc(15), -2.0 / 128.0, op0=OP.mult, op1=OP.mult),
          reads=["h2", "v15"], writes=["ang"])
        A_(lambda a: a.activation(ang, ang, AF.Exp), reads=["ang"], writes=["ang"])
        V(lambda v: v.tensor_scalar(ang, ang, vc(14), None, op0=OP.mult),
          reads=["ang", "v14"], writes=["ang"])

        def sin_reduced(extra_quarter, out_ap, tag):
            V(lambda v: v.tensor_scalar(angw, ang, INV2PI, 0.5 + (0.25 if extra_quarter else 0.0),
                                        op0=OP.mult, op1=OP.add), reads=["ang"], writes=["angw"])
            V(lambda v: v.tensor_copy(angc, angw), reads=["angw"], writes=["angc"])
            for bit in (16.0, 8.0, 4.0, 2.0, 1.0):
                V(lambda v, bit=bit: v.tensor_scalar(tmp64, angc, bit, bit, op0=OP.is_ge, op1=OP.mult),
                  reads=["angc"], writes=["t64"])
                V(lambda v: v.tensor_tensor(angc, angc, tmp64, op=OP.subtract),
                  reads=["angc", "t64"], writes=["angc"])
            V(lambda v: v.tensor_tensor(angw, angw, angc, op=OP.subtract),
              reads=["angw", "angc"], writes=["angw"])
            V(lambda v: v.scalar_tensor_tensor(angc, angw, -CW1, ang, op0=OP.mult, op1=OP.add),
              reads=["angw", "ang"], writes=["angc"])
            V(lambda v: v.scalar_tensor_tensor(angc, angw, -CW2, angc, op0=OP.mult, op1=OP.add),
              reads=["angw", "angc"], writes=["angc"])
            V(lambda v: v.scalar_tensor_tensor(angc, angw, -CW3, angc, op0=OP.mult, op1=OP.add),
              reads=["angw", "angc"], writes=["angc"])
            if extra_quarter:
                V(lambda v: v.tensor_scalar(angc, angc, 1.0, HP1, op0=OP.mult, op1=OP.add),
                  reads=["angc"], writes=["angc"])
                V(lambda v: v.tensor_scalar(angc, angc, 1.0, HP2, op0=OP.mult, op1=OP.add),
                  reads=["angc"], writes=["angc"])
            V(lambda v: v.tensor_scalar(angc, angc, PI32, None, op0=OP.min), reads=["angc"], writes=["angc"])
            V(lambda v: v.tensor_scalar(angc, angc, -PI32, None, op0=OP.max), reads=["angc"], writes=["angc"])
            A_(lambda a: a.activation(out_ap, angc, AF.Sin), reads=["angc"], writes=[tag])

        sin_reduced(False, sinv, "sinv")
        sin_reduced(True, cosv, "cosv")
        for hh in range(16):
            base = hh * 128
            V(lambda v, base=base: v.tensor_copy(
                tabC[:, base:base + 128].rearrange("p (i two) -> p i two", two=2)[:, :, 0:1].squeeze(2), cosv),
              reads=["cosv"], writes=["tabC"])
            V(lambda v, base=base: v.tensor_copy(
                tabC[:, base:base + 128].rearrange("p (i two) -> p i two", two=2)[:, :, 1:2].squeeze(2), cosv),
              reads=["cosv"], writes=["tabC"])
            V(lambda v, base=base: v.tensor_copy(
                tabS[:, base:base + 128].rearrange("p (i two) -> p i two", two=2)[:, :, 0:1].squeeze(2), sinv),
              reads=["sinv"], writes=["tabS"])
            V(lambda v, base=base: v.tensor_copy(
                tabS[:, base:base + 128].rearrange("p (i two) -> p i two", two=2)[:, :, 1:2].squeeze(2), sinv),
              reads=["sinv"], writes=["tabS"])
        V(lambda v: v.tensor_scalar(
            ksw.rearrange("p (i two) -> p i two", two=2)[:, :, 0:1].squeeze(2),
            kpre.rearrange("p (i two) -> p i two", two=2)[:, :, 1:2].squeeze(2), -1.0, None, op0=OP.mult),
          reads=["h1"], writes=["ksw"])
        V(lambda v: v.tensor_copy(
            ksw.rearrange("p (i two) -> p i two", two=2)[:, :, 1:2].squeeze(2),
            kpre.rearrange("p (i two) -> p i two", two=2)[:, :, 0:1].squeeze(2)),
          reads=["h1"], writes=["ksw"])
        V(lambda v: v.tensor_tensor(lnp, kpre, tabC, op=OP.mult), reads=["h1", "tabC"], writes=["lnp2"])
        V(lambda v: v.tensor_tensor(ksw, ksw, tabS, op=OP.mult), reads=["ksw", "tabS"], writes=["ksw"])
        V(lambda v: v.tensor_tensor(lnp, lnp, ksw, op=OP.add), reads=["lnp2", "ksw"], writes=["lnp2"])

        transpose16(pA, "pA", lnp, "lnp2")
        hilo(krth, krtl, pA[:], "krth", "krtl", "pA")

        group("T", [(lambda e, hh=hh: e.transpose(pA[:, hh * 64:(hh + 1) * 64],
                                                  scrB[0:64, hh * 128:(hh + 1) * 128], idf[0:64, 0:64]))
                    for hh in range(16)], reads=["scrB", "idf"], writes=["pA"])
        hilo(qth, qtl, pA[:, 0:1024], "qth", "qtl", "pA")
        hilo(vhh, vll, vpre, "vhh", "vll", "h1")

        fns = []
        for hh in range(16):
            so = slice(hh * 128, (hh + 1) * 128)
            qo = slice(hh * 64, (hh + 1) * 64)
            fns.append(lambda e, so=so, qo=qo: e.matmul(pB[0:64, so], qth[:, qo], krth[:, so], start=True, stop=False))
            fns.append(lambda e, so=so, qo=qo: e.matmul(pB[0:64, so], qth[:, qo], krtl[:, so], start=False, stop=False))
            fns.append(lambda e, so=so, qo=qo: e.matmul(pB[0:64, so], qtl[:, qo], krth[:, so], start=False, stop=True))
        group("T", fns, reads=["qth", "qtl", "krth", "krtl"], writes=["pB"])

        V(lambda v: v.memset(scrB[0:64, 0:2048], NEG), writes=["scrB"])
        V(lambda v: v.tensor_scalar(scrA[0:64, 0:2048], pB[0:64, 0:2048], SCALE, None, op0=OP.mult),
          reads=["pB"], writes=["scrA"])
        wpre = h1[:, 6144:8192]
        V(lambda v: v.select(wpre[0:64, 0:2048].rearrange("p (h c) -> p h c", h=16),
                             masku[:].unsqueeze(1).broadcast_to((64, 16, 128)),
                             scrA[0:64, 0:2048].rearrange("p (h c) -> p h c", h=16),
                             scrB[0:64, 0:2048].rearrange("p (h c) -> p h c", h=16)),
          reads=["scrA", "scrB", "masku"], writes=["wpre"])
        V(lambda v: v.tensor_reduce(vh16[0:64, :], wpre[0:64, 0:2048].rearrange("p (h c) -> p h c", h=16),
                                    axis=AXX, op=OP.max),
          reads=["wpre"], writes=["vh16"])
        V(lambda v: v.tensor_tensor(wpre[0:64, 0:2048].rearrange("p (h c) -> p h c", h=16),
                                    wpre[0:64, 0:2048].rearrange("p (h c) -> p h c", h=16),
                                    vh16[0:64, :].unsqueeze(2).broadcast_to((64, 16, 128)),
                                    op=OP.subtract),
          reads=["wpre", "vh16"], writes=["wpre"])
        A_(lambda a: a.activation(wpre[0:64, 0:2048], wpre[0:64, 0:2048], AF.Exp),
           reads=["wpre"], writes=["wpre"])
        V(lambda v: v.tensor_reduce(vs16[0:64, :], wpre[0:64, 0:2048].rearrange("p (h c) -> p h c", h=16),
                                    axis=AXX, op=OP.add),
          reads=["wpre"], writes=["vs16"])
        V(lambda v: v.reciprocal(vs16[0:64, :], vs16[0:64, :]), reads=["vs16"], writes=["vs16"])
        V(lambda v: v.tensor_tensor(wpre[0:64, 0:2048].rearrange("p (h c) -> p h c", h=16),
                                    wpre[0:64, 0:2048].rearrange("p (h c) -> p h c", h=16),
                                    vs16[0:64, :].unsqueeze(2).broadcast_to((64, 16, 128)),
                                    op=OP.mult),
          reads=["wpre", "vs16"], writes=["wpre"])
        group("T", [(lambda e, hh=hh: e.transpose(pA[:, hh * 64:(hh + 1) * 64],
                                                  wpre[0:64, hh * 128:(hh + 1) * 128], idf[0:64, 0:64]))
                    for hh in range(16)], reads=["wpre", "idf"], writes=["pA"])
        hilo(wth, wtl, pA[:, 0:1024], "wth", "wtl", "pA")
        fns = []
        for hh in range(16):
            so = slice(hh * 128, (hh + 1) * 128)
            qo = slice(hh * 64, (hh + 1) * 64)
            fns.append(lambda e, so=so, qo=qo: e.matmul(pB[0:64, so], wth[:, qo], vhh[:, so], start=True, stop=False))
            fns.append(lambda e, so=so, qo=qo: e.matmul(pB[0:64, so], wth[:, qo], vll[:, so], start=False, stop=False))
            fns.append(lambda e, so=so, qo=qo: e.matmul(pB[0:64, so], wtl[:, qo], vhh[:, so], start=False, stop=True))
        group("T", fns, reads=["wth", "wtl", "vhh", "vll"], writes=["pB"])
        hilo(atnh[:], atnl[:], pB[0:64, 0:2048], "atnh", "atnl", "pB")
        tap("attnh", atnh[:], ["atnh"], dty=BF16)
        tap("attnl", atnl[:], ["atnl"], dty=BF16)
        tap("kpre", kpre, ["h1"])
        tap("vpre", vpre, ["h1"])
        tap("krroped", lnp, ["lnp2"])
        tap("wsm", wpre[0:64, 0:2048], ["wpre"])

        RS = [0]

        def ring_load(src_ap, cols):
            s = RS[0] % 2
            RS[0] += 1
            key = f"ring{s}"
            dma(ring[:, s, 0:cols], src_ap, writes=[key])
            return ring[:, s, :], key

        for g in range(NG):
            tl = [g * GT, g * GT + 1]
            for i in range(GT):
                dma(xb[i][:], xs[tl[i] * 128:(tl[i] + 1) * 128, :], writes=[xkeys[i]], q="G")

            for i in range(GT):
                ln_stats(xb[i][:], xkeys[i], 6 + i, 8 + i, 10 + i, 12 + i)
                ln_apply(scrA[:], "scrA", xb[i][:], xkeys[i], 10 + i, 12 + i)
                if g == 0 and i == DBG_TILE:
                    tap("ln1", scrA[:], ["scrA"])
                transpose16(psums[i], pkeys[i], scrA[:], "scrA")
                hilo(ltH[i][:], ltL[i][:], psums[i][:], f"lt{i}h", f"lt{i}l", pkeys[i])

            for sidx in range(8):
                slot, skey = ring_load(scrWg[sidx], 4096)
                fns = []
                for e2 in range(2):
                    kt = sidx * 2 + e2
                    for i in range(GT):
                        for n in range(2):
                            po = pA[:, i * 1024 + n * 512:i * 1024 + (n + 1) * 512]
                            wh_ = slot[:, e2 * 2048 + n * 512:e2 * 2048 + (n + 1) * 512]
                            wl_ = slot[:, e2 * 2048 + 1024 + n * 512:e2 * 2048 + 1024 + (n + 1) * 512]
                            fns.append(lambda e, po=po, i=i, wh_=wh_, kt=kt: e.matmul(
                                po, ltH[i][:, kt * 128:(kt + 1) * 128], wh_, start=(kt == 0), stop=False))
                            fns.append(lambda e, po=po, i=i, wl_=wl_, kt=kt: e.matmul(
                                po, ltH[i][:, kt * 128:(kt + 1) * 128], wl_, start=False, stop=False))
                            fns.append(lambda e, po=po, i=i, wh_=wh_, kt=kt: e.matmul(
                                po, ltL[i][:, kt * 128:(kt + 1) * 128], wh_, start=False, stop=(kt == 15)))
                group("T", fns, reads=[skey, "lt0h", "lt0l", "lt1h", "lt1l"], writes=["pA"])

            for i in range(GT):
                gsl = slice(i * 1024, (i + 1) * 1024)
                V(lambda v, gsl=gsl: v.tensor_reduce(
                    vh16[:], pA[:, gsl].rearrange("p (h c) -> p h c", h=16), axis=AXX, op=OP.max),
                  reads=["pA"], writes=["vh16"])
                V(lambda v, gsl=gsl: v.tensor_tensor(
                    scrB[:, 0:1024].rearrange("p (h c) -> p h c", h=16),
                    pA[:, gsl].rearrange("p (h c) -> p h c", h=16),
                    vh16[:].unsqueeze(2).broadcast_to((128, 16, 64)), op=OP.subtract),
                  reads=["pA", "vh16"], writes=["scrB"])
                A_(lambda a: a.activation(scrB[:, 0:1024], scrB[:, 0:1024], AF.Exp, scale=SCALE),
                   reads=["scrB"], writes=["scrB"])
                V(lambda v: v.tensor_reduce(vs16[:], scrB[:, 0:1024].rearrange("p (h c) -> p h c", h=16),
                                            axis=AXX, op=OP.add),
                  reads=["scrB"], writes=["vs16"])
                V(lambda v: v.reciprocal(vs16[:], vs16[:]), reads=["vs16"], writes=["vs16"])
                V(lambda v: v.tensor_tensor(
                    scrB[:, 0:1024].rearrange("p (h c) -> p h c", h=16),
                    scrB[:, 0:1024].rearrange("p (h c) -> p h c", h=16),
                    vs16[:].unsqueeze(2).broadcast_to((128, 16, 64)), op=OP.mult),
                  reads=["scrB", "vs16"], writes=["scrB"])
                group("T", [(lambda e, hh=hh: e.transpose(pB[0:64, hh * 128:(hh + 1) * 128],
                                                          scrB[:, hh * 64:(hh + 1) * 64], idf[:]))
                            for hh in range(16)], reads=["scrB", "idf"], writes=["pB"])
                hilo(gth, gtl, pB[0:64, 0:2048], "xq0", "xq1", "pB")
                fns = []
                for hh in range(16):
                    so = slice(hh * 128, (hh + 1) * 128)
                    fns.append(lambda e, so=so: e.matmul(pB[:, so], atnh[:, so], gth[:, so], start=True, stop=False))
                    fns.append(lambda e, so=so: e.matmul(pB[:, so], atnh[:, so], gtl[:, so], start=False, stop=False))
                    fns.append(lambda e, so=so: e.matmul(pB[:, so], atnl[:, so], gth[:, so], start=False, stop=True))
                group("T", fns, reads=["atnh", "atnl", "xq0", "xq1"], writes=["pB"])
                if g == 0 and i == DBG_TILE:
                    tap("gnorm", scrB[:, 0:1024], ["scrB"])
                hilo(wtH[i][:], wtL[i][:], pB[:], f"wt{i}h", f"wt{i}l", "pB")
                if g == 0 and i == DBG_TILE:
                    tap("wth", wtH[i][:], [f"wt{i}h"], dty=BF16)
                    tap("wtl", wtL[i][:], [f"wt{i}l"], dty=BF16)

            for kt in range(16):
                slot, skey = ring_load(scrWp[kt], 4096)
                fns = []
                for i in range(GT):
                    for n in range(4):
                        po = psums[i][:, n * 512:(n + 1) * 512]
                        wh_ = slot[:, n * 512:(n + 1) * 512]
                        wl_ = slot[:, 2048 + n * 512:2048 + (n + 1) * 512]
                        fns.append(lambda e, po=po, i=i, wh_=wh_, kt=kt: e.matmul(
                            po, wtH[i][:, kt * 128:(kt + 1) * 128], wh_, start=(kt == 0), stop=False))
                        fns.append(lambda e, po=po, i=i, wl_=wl_, kt=kt: e.matmul(
                            po, wtH[i][:, kt * 128:(kt + 1) * 128], wl_, start=False, stop=False))
                        fns.append(lambda e, po=po, i=i, wh_=wh_, kt=kt: e.matmul(
                            po, wtL[i][:, kt * 128:(kt + 1) * 128], wh_, start=False, stop=(kt == 15)))
                group("T", fns, reads=[skey, "wt0h", "wt0l", "wt1h", "wt1l"], writes=["pA", "pB"])

            for i in range(GT):
                V(lambda v, i=i: v.tensor_tensor(xb[i][:], xb[i][:], psums[i][:], op=OP.add),
                  reads=[xkeys[i], pkeys[i]], writes=[xkeys[i]])
                if g == 0 and i == DBG_TILE:
                    tap("x1", xb[i][:], [xkeys[i]])
            for i in range(GT):
                ln_stats(xb[i][:], xkeys[i], 6 + i, 8 + i, 10 + i, 12 + i)
                ln_apply(scrA[:], "scrA", xb[i][:], xkeys[i], 10 + i, 12 + i)
                V(lambda v: v.tensor_reduce(vc(14), scrA[:], axis=AXX, op=OP.max, apply_absolute_value=True),
                  reads=["scrA"], writes=["v14"])
                V(lambda v: v.tensor_scalar(vc(14), vc(14), CD1, EPS_Q, op0=OP.mult, op1=OP.max),
                  reads=["v14"], writes=["v14"])
                V(lambda v: v.reciprocal(vc(15), vc(14)), reads=["v14"], writes=["v15"])
                V(lambda v: v.tensor_scalar(vc(15), vc(15), 127.0, None, op0=OP.mult),
                  reads=["v15"], writes=["v15"])
                V(lambda v, i=i: v.reciprocal(vc(16 + i), vc(15)), reads=["v15"], writes=[f"al{i}"])
                V(lambda v, i=i: v.tensor_tensor(vc(16 + i), vc(16 + i), vc(4), op=OP.mult),
                  reads=[f"al{i}", "s1"], writes=[f"al{i}"])
                V(lambda v, i=i: v.reciprocal(vc(18 + i), vc(16 + i)), reads=[f"al{i}"], writes=[f"rc{i}"])
                V(lambda v: v.tensor_scalar(vc(15), vc(15), CD1, None, op0=OP.mult),
                  reads=["v15"], writes=["v15"])
                V(lambda v: v.tensor_scalar(scrB[:], scrA[:], vc(15), 128.5, op0=OP.mult, op1=OP.add),
                  reads=["scrA", "v15"], writes=["scrB"])
                cbuf = htiles[i][:, 0:2048]
                for bit in (128.0, 64.0, 32.0, 16.0, 8.0, 4.0, 2.0, 1.0):
                    V(lambda v, bit=bit, cbuf=cbuf: v.tensor_scalar(cbuf, scrB[:], bit, bit, op0=OP.is_ge, op1=OP.mult),
                      reads=["scrB"], writes=[hkeys[i]])
                    V(lambda v, cbuf=cbuf: v.tensor_tensor(scrB[:], scrB[:], cbuf, op=OP.subtract),
                      reads=["scrB", hkeys[i]], writes=["scrB"])
                bq = htiles[i][:, 2048:4096]
                V(lambda v, bq=bq: v.scalar_tensor_tensor(bq, scrA[:], vc(15), scrB[:], op0=OP.mult, op1=OP.subtract),
                  reads=["scrA", "v15", "scrB"], writes=[hkeys[i]])
                V(lambda v, bq=bq: v.tensor_scalar(bq, bq, 1.0, 0.5, op0=OP.mult, op1=OP.add),
                  reads=[hkeys[i]], writes=[hkeys[i]])
                if g == 0 and i == DBG_TILE:
                    tap("xqf", bq, [hkeys[i]])
                transpose16(psums[1 - i], pkeys[1 - i], bq, hkeys[i])
                V(lambda v, i=i: v.tensor_copy(xqt[i][:], psums[1 - i][:]),
                  reads=[pkeys[1 - i]], writes=[f"xq{i}"])
                p.op("T", lambda e, i=i: e.transpose(psums[1 - i][0:1, 0:128], vc(18 + i), idf[:]),
                     reads=[f"rc{i}", "idf", pkeys[1 - i]], writes=[pkeys[1 - i]])
                V(lambda v, i=i: v.tensor_copy(btT[i][0:1, 0:128], psums[1 - i][0:1, 0:128]),
                  reads=[pkeys[1 - i]], writes=[btkeys[i]])
                V(lambda v, i=i: v.tensor_tensor(btT[i][0:1, 128:256], psums[1 - i][0:1, 0:128],
                                                 btT[i][0:1, 0:128], op=OP.subtract),
                  reads=[pkeys[1 - i]], writes=[btkeys[i]])
                for (row, cs) in ((0, 0), (1, 128), (2, 0), (3, 128)):
                    dma(r1ext[i][row:row + 1, :], btT[i][0:1, cs:cs + 128],
                        reads=[btkeys[i]], writes=["r1e"], q="G")

            for oc in range(8):
                for sl4 in range(4):
                    slot, skey = ring_load(scrW1[oc, sl4], 5120)
                    fns = []
                    for e2 in range(4):
                        kt = sl4 * 4 + e2
                        for i in range(GT):
                            for n in range(2):
                                po = psums[i][:, (oc % 2) * 1024 + n * 512:(oc % 2) * 1024 + (n + 1) * 512]
                                fns.append(lambda e, po=po, i=i, slot=slot, e2=e2, n=n, kt=kt: e.matmul(
                                    po, xqt[i][:, kt * 128:(kt + 1) * 128],
                                    slot[:, e2 * 1024 + n * 512:e2 * 1024 + (n + 1) * 512],
                                    start=(kt == 0), stop=False))
                    if sl4 == 3:
                        for i in range(GT):
                            for n in range(2):
                                po = psums[i][:, (oc % 2) * 1024 + n * 512:(oc % 2) * 1024 + (n + 1) * 512]
                                fns.append(lambda e, po=po, i=i, slot=slot, n=n: e.matmul(
                                    po, r1ext[i][0:4, :],
                                    slot[0:4, 4096 + n * 512:4096 + (n + 1) * 512],
                                    start=False, stop=True))
                    group("T", fns, reads=[skey, "xq0", "xq1", "r1e"], writes=["pA", "pB"])
                for i in range(GT):
                    hsl = slice(oc * 1024, (oc + 1) * 1024)
                    V(lambda v, i=i, hsl=hsl, oc=oc: v.tensor_scalar(
                        htiles[i][:, hsl], psums[i][:, (oc % 2) * 1024:(oc % 2) * 1024 + 1024],
                        vc(16 + i), None, op0=OP.mult),
                      reads=[pkeys[i], f"al{i}"], writes=[hkeys[i]])

            for i in range(GT):
                if g == 0 and i == DBG_TILE:
                    tap("hpre", htiles[i][:, 0:2048], [hkeys[i]])
                A_(lambda a, i=i: a.activation(htiles[i][:], htiles[i][:], AF.Gelu),
                   reads=[hkeys[i]], writes=[hkeys[i]])
                if g == 0 and i == DBG_TILE:
                    tap("hgelu", htiles[i][:, 0:2048], [hkeys[i]])
                A_(lambda a, i=i: a.activation(btT[i][:], htiles[i][:], AF.Square, accum_out=vc(14)),
                   reads=[hkeys[i]], writes=[btkeys[i], "v14"])
                V(lambda v, i=i: v.tensor_reduce(vc(15), htiles[i][:], axis=AXX, op=OP.max,
                                                 apply_absolute_value=True),
                  reads=[hkeys[i]], writes=["v15"])
                V(lambda v: v.tensor_scalar(vc(14), vc(14), 1.0 / 8192.0, None, op0=OP.mult),
                  reads=["v14"], writes=["v14"])
                V(lambda v: v.tensor_scalar(vc(14), vc(14), 1.0, EPS_RMS, op0=OP.mult, op1=OP.add),
                  reads=["v14"], writes=["v14"])
                A_(lambda a: a.activation(vc(14), vc(14), AF.Ln), reads=["v14"], writes=["v14"])
                A_(lambda a: a.activation(vc(14), vc(14), AF.Exp, scale=-0.5), reads=["v14"], writes=["v14"])
                V(lambda v: v.tensor_scalar(vc(14), vc(14), CD2, None, op0=OP.mult),
                  reads=["v14"], writes=["v14"])
                V(lambda v: v.tensor_tensor(vc(15), vc(15), vc(14), op=OP.mult),
                  reads=["v14", "v15"], writes=["v15"])
                V(lambda v: v.tensor_scalar(vc(15), vc(15), 1.0, EPS_Q, op0=OP.mult, op1=OP.max),
                  reads=["v15"], writes=["v15"])
                V(lambda v: v.reciprocal(vc(15), vc(15)), reads=["v15"], writes=["v15"])
                V(lambda v: v.tensor_scalar(vc(15), vc(15), 127.0, None, op0=OP.mult),
                  reads=["v15"], writes=["v15"])
                V(lambda v, i=i: v.reciprocal(vc(16 + i), vc(15)), reads=["v15"], writes=[f"al{i}"])
                V(lambda v, i=i: v.tensor_tensor(vc(16 + i), vc(16 + i), vc(5), op=OP.mult),
                  reads=[f"al{i}", "s2"], writes=[f"al{i}"])
                V(lambda v, i=i: v.reciprocal(vc(18 + i), vc(16 + i)), reads=[f"al{i}"], writes=[f"rc{i}"])
                V(lambda v: v.tensor_tensor(vc(15), vc(15), vc(14), op=OP.mult),
                  reads=["v15", "v14"], writes=["v15"])
                p.op("T", lambda e, i=i: e.transpose(psums[i][0:1, 0:128], vc(18 + i), idf[:]),
                     reads=[f"rc{i}", "idf", pkeys[i]], writes=[pkeys[i]])
                V(lambda v, i=i: v.tensor_copy(btT[i][0:1, 0:128], psums[i][0:1, 0:128]),
                  reads=[pkeys[i]], writes=[btkeys[i]])
                V(lambda v, i=i: v.tensor_tensor(btT[i][0:1, 128:256], psums[i][0:1, 0:128],
                                                 btT[i][0:1, 0:128], op=OP.subtract),
                  reads=[pkeys[i]], writes=[btkeys[i]])
                for (row, cs) in ((0, 0), (1, 128), (2, 0), (3, 128)):
                    dma(r2ext[i][row:row + 1, :], btT[i][0:1, cs:cs + 128],
                        reads=[btkeys[i]], writes=["r2e"], q="G")
                for cc in range(4):
                    hs = slice(cc * 2048, (cc + 1) * 2048)
                    V(lambda v, i=i, hs=hs: v.tensor_scalar(scrA[:], htiles[i][:, hs], vc(15), 128.5,
                                                            op0=OP.mult, op1=OP.add),
                      reads=[hkeys[i], "v15"], writes=["scrA"])
                    for bit in (128.0, 64.0, 32.0, 16.0, 8.0, 4.0, 2.0, 1.0):
                        V(lambda v, bit=bit: v.tensor_scalar(scrB[:], scrA[:], bit, bit, op0=OP.is_ge, op1=OP.mult),
                          reads=["scrA"], writes=["scrB"])
                        V(lambda v: v.tensor_tensor(scrA[:], scrA[:], scrB[:], op=OP.subtract),
                          reads=["scrA", "scrB"], writes=["scrA"])
                    V(lambda v, i=i, hs=hs: v.scalar_tensor_tensor(scrB[:], htiles[i][:, hs], vc(15), scrA[:],
                                                                   op0=OP.mult, op1=OP.subtract),
                      reads=[hkeys[i], "v15", "scrA"], writes=["scrB"])
                    V(lambda v: v.tensor_scalar(scrB[:], scrB[:], 1.0, 0.5, op0=OP.mult, op1=OP.add),
                      reads=["scrB"], writes=["scrB"])
                    if g == 0 and i == DBG_TILE and cc == 0:
                        tap("b2q", scrB[:], ["scrB"])
                    transpose16(psums[i], pkeys[i], scrB[:], "scrB")
                    V(lambda v, i=i, cc=cc: v.tensor_copy(btT[i][:, cc * 2048:(cc + 1) * 2048], psums[i][:]),
                      reads=[pkeys[i]], writes=[btkeys[i]])

            for sidx in range(32):
                slot, skey = ring_load(scrW2[sidx], 4096)
                fns = []
                for e2 in range(2):
                    kt = sidx * 2 + e2
                    for i in range(GT):
                        for n in range(4):
                            po = psums[i][:, n * 512:(n + 1) * 512]
                            fns.append(lambda e, po=po, i=i, slot=slot, e2=e2, n=n, kt=kt: e.matmul(
                                po, btT[i][:, kt * 128:(kt + 1) * 128],
                                slot[:, e2 * 2048 + n * 512:e2 * 2048 + (n + 1) * 512],
                                start=(kt == 0), stop=False))
                if sidx == 31:
                    for i in range(GT):
                        for n in range(4):
                            po = psums[i][:, n * 512:(n + 1) * 512]
                            fns.append(lambda e, po=po, i=i, n=n: e.matmul(
                                po, r2ext[i][0:4, :], b2ext[0:4, n * 512:(n + 1) * 512],
                                start=False, stop=True))
                group("T", fns, reads=[skey, "bt1", "bt2", "r2e", "b2e"], writes=["pA", "pB"])

            for i in range(GT):
                V(lambda v, i=i: v.scalar_tensor_tensor(xb[i][:], psums[i][:], vc(16 + i), xb[i][:],
                                                        op0=OP.mult, op1=OP.add),
                  reads=[pkeys[i], f"al{i}", xkeys[i]], writes=[xkeys[i]])
                if g == 0 and i == DBG_TILE:
                    tap("x2", xb[i][:], [xkeys[i]])
                dma(y[tl[i] * 128:(tl[i] + 1) * 128, :], xb[i][:], reads=[xkeys[i]], q="G")

        p.raw_wait("S", ("G", p.cnt["G"]))
        p.raw_wait("S", ("V", p.cnt["V"]))

        @block.sync
        def _(sync):
            p.replay(sync, "S", sems)

        @block.vector
        def _(v):
            p.replay(v, "V", sems)

        @block.scalar
        def _(a):
            p.replay(a, "A", sems)

        @block.tensor
        def _(t):
            p.replay(t, "T", sems)

        @block.gpsimd
        def _(gp):
            p.replay(gp, "G", sems)

    nc._dbg_outs = dbg_outs
    return nc


_NC_CACHE = {}


def kernel(L0_attn_x, L0_attn_latent_queries, L0_attn_Wk, L0_attn_Wv, L0_attn_Wg,
           L0_attn_Wp, L0_ffn_w1, L0_ffn_b1, L0_ffn_w2, L0_ffn_b2):
    x = np.ascontiguousarray(np.asarray(L0_attn_x, dtype=np.float32))
    latq = np.ascontiguousarray(np.asarray(L0_attn_latent_queries, dtype=np.float32).reshape(64, 2048))
    wk = np.ascontiguousarray(np.asarray(L0_attn_Wk, dtype=np.float32))
    wv = np.ascontiguousarray(np.asarray(L0_attn_Wv, dtype=np.float32))
    wg = np.ascontiguousarray(np.asarray(L0_attn_Wg, dtype=np.float32))
    wp = np.ascontiguousarray(np.asarray(L0_attn_Wp, dtype=np.float32))
    w1 = np.ascontiguousarray(np.asarray(L0_ffn_w1, dtype=np.float32))
    b1 = np.ascontiguousarray(np.asarray(L0_ffn_b1, dtype=np.float32).reshape(1, INNER))
    w2 = np.ascontiguousarray(np.asarray(L0_ffn_w2, dtype=np.float32))
    b2 = np.ascontiguousarray(np.asarray(L0_ffn_b2, dtype=np.float32).reshape(1, C))
    idf = np.eye(128, dtype=np.float32)
    masku = (np.arange(128)[None, :] <= np.arange(64)[:, None]).astype(np.uint8)

    if "nc" not in _NC_CACHE:
        _NC_CACHE["nc"] = build_core()
    nc = _NC_CACHE["nc"]

    in_maps = []
    for c in range(NCORE):
        b, s = c // 4, c % 4
        in_maps.append({
            "xs": np.ascontiguousarray(x[b, s * TOK:(s + 1) * TOK]),
            "xpre": np.ascontiguousarray(x[b, 0:128]),
            "latq": latq, "wk": wk, "wv": wv, "wg": wg, "wp": wp,
            "w1": w1, "b1": b1, "w2": w2, "b2": b2,
            "idf": idf, "masku": masku,
        })
    res = run_bass_kernel_spmd(nc, in_maps, core_ids=list(range(NCORE)))
    out = np.empty((B, T, C), dtype=np.float32)
    for c in range(NCORE):
        b, s = c // 4, c % 4
        out[b, s * TOK:(s + 1) * TOK] = res.results[c]["y"]
    return out
